# version 02
# baseline (speedup 1.0000x reference)
import numpy as np
import concourse.bass as bass
import concourse.mybir as mybir
from concourse.bass_utils import run_bass_kernel_spmd

F32 = mybir.dt.float32
AF = mybir.ActivationFunctionType
ALU = mybir.AluOpType
AX = mybir.AxisListType

B, S, D = 4, 4096, 1024
H_, DH, E = 16, 64, 8
TPC = 2048
NT = TPC // 128
KC = D // 128
NHEAD = 16

DEBUG = 0
PHASE_LIMIT = 9
NHEADS_RUN = NHEAD
P2_N = 6
P2_QNORM = 1
P2_QSTORE = 1
P3_DO_AV = 1
P3_DO_EVAC = 1
P3_SETUP_N = 3


class Sched:
    ENGINES = ("sync", "tensor", "vector", "scalar", "gpsimd")

    def __init__(self):
        self.ops = {e: [] for e in self.ENGINES}
        self.ctr = {}
        self.semnames = set()
        self.marks = {}

    def n(self, sem):
        return self.ctr.get(sem, 0)

    AUTOCHAIN = {"vector": "vec", "scalar": "sact"}

    def emit(self, eng, fn, waits=(), inc=None, incn=1):
        w = [(s, v) for (s, v) in waits if v and v > 0]
        if eng in self.AUTOCHAIN and inc is None:
            inc = self.AUTOCHAIN[eng]
            incn = 1
        self.ops[eng].append((w, fn, inc, incn))
        if inc is not None:
            self.semnames.add(inc)
            self.ctr[inc] = self.ctr.get(inc, 0) + incn
        for s, _ in w:
            self.semnames.add(s)

    def check_runnable(self):
        idx = {e: 0 for e in self.ops}
        cur = {s: 0 for s in self.semnames}
        total = sum(len(v) for v in self.ops.values())
        done = 0
        progress = True
        while progress:
            progress = False
            for e in self.ops:
                while idx[e] < len(self.ops[e]):
                    w, _, inc, incn = self.ops[e][idx[e]]
                    if any(cur.get(s, 0) < v for s, v in w):
                        break
                    if inc is not None:
                        cur[inc] = cur.get(inc, 0) + incn
                    idx[e] += 1
                    done += 1
                    progress = True
        if done != total:
            stuck = {e: (len(self.ops[e]) - idx[e]) for e in self.ops if idx[e] < len(self.ops[e])}
            first = {e: self.ops[e][idx[e]][0] for e in stuck}
            raise RuntimeError(f"schedule deadlock: remaining={stuck} first_waits={first}")

    def realize(self, nc, sems):
        ctx = nc.Block()
        block = ctx.__enter__()
        try:
            for eng in self.ENGINES:
                ops = self.ops[eng]
                if not ops:
                    continue

                chain = Sched.AUTOCHAIN.get(eng)

                def section(engine, ops=ops, chain=chain):
                    hwm = {}
                    idx = 0
                    for w, fn, inc, incn in ops:
                        if chain is not None and idx > 0:
                            w = list(w) + [(chain, idx)]
                        idx += 1
                        for s, v in w:
                            if hwm.get(s, -1) < v:
                                engine.wait_ge(sems[s], v)
                                hwm[s] = v
                        ins = fn(engine)
                        if inc is not None:
                            ins.then_inc(sems[inc], incn)
                getattr(block, eng)(section)
        finally:
            ctx.__exit__(None, None, None)


def build_nc():
    nc = bass.Bass("TRN2", target_bir_lowering=False, dynamic_dma_scratch_size=16384)

    x_in = nc.dram_tensor("x", [TPC, D], F32, kind="ExternalInput").ap()
    wqkv = nc.dram_tensor("wqkv", [D, 3 * D], F32, kind="ExternalInput").ap()
    bqkv = nc.dram_tensor("bqkv", [1, 3 * D], F32, kind="ExternalInput").ap()
    wout_d = nc.dram_tensor("wout", [D, D], F32, kind="ExternalInput").ap()
    bout = nc.dram_tensor("bout", [1, D], F32, kind="ExternalInput").ap()
    relpt = nc.dram_tensor("relpt", [DH, S], F32, kind="ExternalInput").ap()
    wt_d = nc.dram_tensor("wt", [E, D, D], F32, kind="ExternalInput").ap()
    bffn_d = nc.dram_tensor("bffn", [E, D], F32, kind="ExternalInput").ap()
    wgt_d = nc.dram_tensor("wgt", [D, E], F32, kind="ExternalInput").ap()
    bg_d = nc.dram_tensor("bg", [1, E], F32, kind="ExternalInput").ap()
    ident_d = nc.dram_tensor("ident", [128, 128], F32, kind="ExternalInput").ap()
    y_out = nc.dram_tensor("y", [TPC, D], F32, kind="ExternalOutput").ap()
    dbg = {}
    if DEBUG:
        for nm in ("dbg_x1", "dbg_x2"):
            dbg[nm] = nc.dram_tensor(nm, [TPC, D], F32, kind="ExternalOutput").ap()
        dbg["dbg_attnT"] = nc.dram_tensor("dbg_attnT", [D, TPC], F32, kind="ExternalOutput").ap()

    agk_src = nc.dram_tensor("agk_src", [D, TPC], F32).ap()
    agk_dst = nc.dram_tensor("agk_dst", [2 * D, TPC], F32).ap()
    agv_src = nc.dram_tensor("agv_src", [TPC, D], F32).ap()
    agv_dst = nc.dram_tensor("agv_dst", [S, D], F32).ap()
    agn_src = nc.dram_tensor("agn_src", [1, NHEAD], F32).ap()
    agn_dst = nc.dram_tensor("agn_dst", [2, NHEAD], F32).ap()
    qsc = nc.dram_tensor("qsc", [D, TPC], F32).ap()

    sch = Sched()
    RG = [[0, 1], [2, 3], [4, 5], [6, 7]]

    ctxs = []

    def alloc_sb(name, shape, dtype=F32):
        h = nc.sbuf_tensor(name, shape, dtype)
        t = h.__enter__()
        ctxs.append(h)
        return t

    def alloc_ps(name, shape):
        h = nc.psum_tensor(name, shape)
        t = h.__enter__()
        ctxs.append(h)
        return t

    xbuf = alloc_sb("xbuf", [128, 16384])
    ybuf = alloc_sb("ybuf", [128, 16384])
    wbuf = alloc_sb("wbuf", [128, 16384])
    yscr = alloc_sb("yscr", [128, 1024])
    stag = yscr
    stagsq = alloc_sb("stagsq", [128, 512])
    stgo = stagsq
    ident = alloc_sb("identt", [128, 128])
    ones128 = alloc_sb("ones128", [128, 1])
    ones1 = alloc_sb("ones1", [1, 128])
    bqkvT = alloc_sb("bqkvT", [128, 16])
    bvT = alloc_sb("bvT", [64, NHEAD])
    boutrow = yscr[0:1, 0:1024]
    bgrow = alloc_sb("bgrow", [1, E])
    bffn = alloc_sb("bffnsb", [E, D])
    wgt = alloc_sb("wgtsb", [128, KC * E])
    gate = alloc_sb("gatesb", [128, 2 * E])
    whalf = alloc_sb("whalfsb", [128, NT * E])
    whalfT = stagsq[0:8, 0:128]
    keep = alloc_sb("keepm", [128, E])
    lnst = alloc_sb("lnst", [128, 8])
    epsv = alloc_sb("epsv", [128, 1])
    nrm = alloc_sb("nrm", [1, 256])
    negc = alloc_sb("negc", [128, 64])
    onesr = alloc_sb("onesr", [128, 64])

    psum = alloc_ps("ps", [128, 4096])

    def bank(i, lo=0, hi=512):
        return psum[:, i * 512 + lo: i * 512 + hi]

    E_ = sch.emit
    n_ = sch.n
    M = sch.marks

    x_view = x_in.rearrange("(i p) c -> p i c", p=128)
    xb_t = xbuf.ap().rearrange("p (i c) -> p i c", c=1024)
    E_("sync", lambda e: e.dma_start(out=xb_t, in_=x_view), inc="ldx", incn=16)
    E_("sync", lambda e: e.dma_start(out=ident[:, :], in_=ident_d), inc="ldc", incn=16)
    E_("sync", lambda e: e.dma_start(out=bqkvT[:, :], in_=bqkv[0:1, 0:2048].rearrange("o (c p) -> (o p) c", p=128), allow_slow_non_contiguous=True), inc="ldc", incn=16)
    E_("sync", lambda e: e.dma_start(out=bvT[:, :], in_=bqkv[0:1, 2048:3072].rearrange("o (h p) -> (o p) h", p=64), allow_slow_non_contiguous=True), inc="ldc", incn=16)
    E_("sync", lambda e: e.dma_start(out=bgrow[0:1, :], in_=bg_d), inc="ldc", incn=16)
    E_("sync", lambda e: e.dma_start(out=bffn[:, :], in_=bffn_d), inc="ldc", incn=16)
    E_("sync", lambda e: e.dma_start(out=wgt[:, :].rearrange("p (k e) -> p k e", e=E), in_=wgt_d.rearrange("(k p) e -> p k e", p=128)), inc="ldc", incn=16)
    E_("vector", lambda e: e.memset(ones128[:, :], 1.0), inc="vec")
    E_("vector", lambda e: e.memset(ones1[0:1, :], 1.0), inc="vec")
    E_("vector", lambda e: e.memset(epsv[:, :], 1e-5), inc="vec")
    if PHASE_LIMIT >= 3:
        E_("vector", lambda e: e.memset(onesr[:, :], 1.0), inc="vec")
    LD_ALL = n_("ldc")

    def ln_tile(xi, waits_first):
        sums, sumsq, mean, ex2, negv, rstd = (lnst[:, j:j + 1] for j in range(6))
        w0 = list(waits_first) + [("sact", n_("sact")), ("tmm", n_("tmm"))]
        E_("vector", lambda e, xi=xi: e.reduce_sum(sums, xi, axis=AX.X), waits=w0, inc="vec")
        E_("scalar", lambda e, xi=xi: e.activation(yscr[:, :], xi, AF.Square, accum_out=sumsq),
           waits=list(waits_first) + [("tmm", n_("tmm")), ("vec", n_("vec") - 1)], inc="sact")
        swt = n_("sact")
        E_("vector", lambda e: e.tensor_scalar(mean, sums, 1.0 / 1024.0, None, op0=ALU.mult), inc="vec")
        E_("vector", lambda e: e.tensor_scalar(ex2, sumsq, 1.0 / 1024.0, None, op0=ALU.mult),
           waits=[("sact", swt)], inc="vec")
        E_("vector", lambda e: e.scalar_tensor_tensor(negv, mean, mean[:, 0:1], ex2, op0=ALU.mult, op1=ALU.subtract), inc="vec")
        E_("scalar", lambda e: e.activation(rstd, negv, AF.Sqrt, bias=epsv[:, 0:1], scale=-1.0),
           waits=[("vec", n_("vec"))], inc="sact")
        E_("vector", lambda e: e.reciprocal(negv, rstd), waits=[("sact", n_("sact"))], inc="vec")
        E_("vector", lambda e, xi=xi: e.tensor_scalar(yscr[:, :], xi, mean[:, 0:1], negv[:, 0:1], op0=ALU.subtract, op1=ALU.mult),
           inc="vec")
        yv = n_("vec")
        tw = [("vec", yv), ("ldc", LD_ALL), ("vec", M.get("ytcopy", 0))]
        for c in range(KC):
            bk = bank(0) if c < 4 else bank(1)
            sub = bk[:, (c % 4) * 128:(c % 4) * 128 + 128]
            E_("tensor", lambda e, s=sub, c=c: e.transpose(s, yscr[:, c * 128:(c + 1) * 128], ident[:, :]),
               waits=tw if c == 0 else (), inc="tmm")
        return n_("tmm")

    def yt_copy(yt, i, tt):
        for half in range(2):
            def cp(e, i=i, half=half):
                outap = yt[:, 4 * half:4 * half + 4, i * 128:(i + 1) * 128]
                inap = bank(half)[:, :].rearrange("p (c t) -> p c t", t=128)
                return e.tensor_copy(outap, inap)
            E_("vector", cp, waits=[("tmm", tt)], inc="vec")
        M["ytcopy"] = n_("vec")
        return n_("vec")

    def ffn_phase(xsrc, ytb, wstream, wait_x):
        yt = ytb.ap().rearrange("p (k t) -> p k t", t=TPC)
        wst = wstream.ap().rearrange("p (b k o) -> p b k o", b=2, o=1024)

        for i in range(NT):
            xi = xsrc[:, i * 1024:(i + 1) * 1024]
            tt = ln_tile(xi, wait_x)
            ytv = yt_copy(yt, i, tt)
            gb = bank(2, 0, 8)
            gw = [("vec", ytv), ("ldc", LD_ALL), ("sact", M.get(("gexp", i - 1), 0)),
                  ("vec", M.get(("gred", i - 1), 0)), ("vec", M.get(("biasv", i - 1), 0))]
            for k in range(KC):
                E_("tensor", lambda e, k=k, i=i: e.matmul(gb, yt[:, k, i * 128:(i + 1) * 128],
                                                          wgt[:, k * E:(k + 1) * E], start=(k == 0), stop=False),
                   waits=gw if k == 0 else (), inc=None)
            E_("tensor", lambda e: e.matmul(gb, ones1[0:1, :], bgrow[0:1, :], start=False, stop=True), inc="tmm")
            gt = n_("tmm")
            sums, mean = lnst[:, 0:1], lnst[:, 2:3]
            m1n, p1, p2 = lnst[:, 6:7], lnst[:, 7:8], lnst[:, 6:7]
            eg = gate[:, 0:E]
            pr = gate[:, E:2 * E]
            E_("vector", lambda e: e.reduce_max(m1n, gb, axis=AX.X, negate=True), waits=[("tmm", gt)], inc="vec")
            M[("gred", i)] = n_("vec")
            E_("scalar", lambda e, eg=eg: e.activation(eg, gb, AF.Exp, bias=m1n[:, 0:1], accum_out=sums),
               waits=[("vec", n_("vec")), ("tmm", gt)], inc="sact")
            M[("gexp", i)] = n_("sact")
            E_("vector", lambda e: e.reciprocal(mean, sums), waits=[("sact", n_("sact"))], inc="vec")
            E_("vector", lambda e, eg=eg, pr=pr: e.tensor_scalar(pr, eg, mean[:, 0:1], None, op0=ALU.mult), inc="vec")
            E_("vector", lambda e, pr=pr: e.reduce_max(p1, pr, axis=AX.X), inc="vec")
            E_("vector", lambda e, pr=pr: e.tensor_scalar(keep[:, :], pr, p1[:, 0:1], None, op0=ALU.is_ge), inc="vec")
            E_("vector", lambda e, pr=pr, eg=eg: e.scalar_tensor_tensor(eg, keep[:, :], -3.0, pr, op0=ALU.mult, op1=ALU.add), inc="vec")
            E_("vector", lambda e, eg=eg: e.reduce_max(p2, eg, axis=AX.X), inc="vec")
            E_("vector", lambda e, pr=pr: e.tensor_scalar(keep[:, :], pr, p2[:, 0:1], None, op0=ALU.is_ge), inc="vec")
            wh = whalf[:, i * E:(i + 1) * E]
            E_("vector", lambda e, pr=pr, wh=wh: e.scalar_tensor_tensor(wh, pr, 0.5, keep[:, :], op0=ALU.mult, op1=ALU.mult), inc="vec")
            whv = n_("vec")
            tb = bank(3)[0:8, 0:128]
            E_("tensor", lambda e, wh=wh, tb=tb: e.transpose(tb, wh, ident[:, :]),
               waits=[("vec", whv), ("vec", M.get(("whtc", i - 1), 0)), ("vec", M.get(("biasv", i - 1), 0))], inc="tmm")
            E_("vector", lambda e, tb=tb: e.tensor_copy(whalfT[:, :], tb),
               waits=[("tmm", n_("tmm"))], inc="vec")
            M[("whtc", i)] = n_("vec")
            bacc = psum[:, 2 * 512:4 * 512]
            bw = [("vec", M[("whtc", i)]), ("sact", M[("gexp", i)]), ("ldc", LD_ALL)]
            for nn2 in range(2):
                E_("tensor", lambda e, nn2=nn2, bacc=bacc:
                   e.matmul(bacc[:, nn2 * 512:(nn2 + 1) * 512], whalfT[:, :],
                            bffn[:, nn2 * 512:(nn2 + 1) * 512], start=True, stop=True),
                   waits=bw if nn2 == 0 else (), inc="tmm" if nn2 == 1 else None)
            bt = n_("tmm")
            E_("vector", lambda e, xi=xi, bacc=bacc: e.tensor_add(xi, xi, bacc), waits=[("tmm", bt)], inc="vec")
            M[("biasv", i)] = n_("vec")
        A_T, A_V = n_("tmm"), n_("vec")

        def wload(e_i):
            par = e_i % 2
            wv = list(wait_x) + ([("tmm", M[("ffnmm", e_i - 2)])] if e_i >= 2 else [])
            E_("sync", lambda e, e_i=e_i, par=par: e.dma_start(
                out=wst[:, par], in_=wt_d[e_i].rearrange("(k p) o -> p k o", p=128)),
               waits=wv, inc=f"w{par}", incn=16)
            M[("wld", e_i)] = n_(f"w{par}")
        wload(0)
        wload(1)
        for e_i in range(E):
            par = e_i % 2
            if e_i + 2 < E:
                pass
            for i in range(NT):
                bp = 4 + 2 * (i % 2)
                acc = psum[:, bp * 512:(bp + 2) * 512]
                idx = e_i * NT + i
                wv = [(f"w{par}", M[("wld", e_i)]),
                      ("vec", A_V if idx < 2 else M[("ffnev", idx - 2)])]
                first = True
                for k in range(KC):
                    for nn2 in range(2):
                        last = (k == KC - 1 and nn2 == 1)
                        E_("tensor", lambda e, k=k, nn2=nn2, i=i, par=par, acc=acc:
                           e.matmul(acc[:, nn2 * 512:(nn2 + 1) * 512], yt[:, k, i * 128:(i + 1) * 128],
                                    wst[:, par, k, nn2 * 512:(nn2 + 1) * 512],
                                    start=(k == 0), stop=(k == KC - 1)),
                           waits=wv if first else (), inc="tmm" if last else None)
                        first = False
                M[("ffnmm", e_i)] = n_("tmm")
                mmd = n_("tmm")
                xi = xsrc[:, i * 1024:(i + 1) * 1024]
                whcol = whalf[:, i * E + e_i: i * E + e_i + 1]
                E_("vector", lambda e, xi=xi, acc=acc, whcol=whcol:
                   e.scalar_tensor_tensor(xi, acc, whcol, xi, op0=ALU.mult, op1=ALU.add),
                   waits=[("tmm", mmd)], inc="vec")
                M[("ffnev", idx)] = n_("vec")
            if e_i + 2 < E:
                wload(e_i + 2)
        return n_("vec")

    P1_V = ffn_phase(xbuf, ybuf, wbuf, [("ldx", 16)])
    P1_T = n_("tmm")

    P2_T = P2_V = P2_SA = Q_ST = 0
    if PHASE_LIMIT >= 2:
        yt = ybuf.ap().rearrange("p (k t) -> p k t", t=TPC)
        for i in range(NT):
            xi = xbuf[:, i * 1024:(i + 1) * 1024]
            tt = ln_tile(xi, [("vec", P1_V), ("tmm", P1_T)])
            yt_copy(yt, i, tt)
            E_("vector", lambda e, i=i, xi=xi: e.tensor_copy(wbuf[:, i * 1024:(i + 1) * 1024], xi), inc="vec")
        P2LN_V = n_("vec")
        if DEBUG:
            E_("gpsimd", lambda e: e.dma_start(out=dbg["dbg_x1"].rearrange("(i p) c -> p i c", p=128),
                                               in_=wbuf.ap().rearrange("p (i c) -> p i c", c=1024)),
               waits=[("vec", P2LN_V)], inc="dbg1", incn=16)
        M["dbg1"] = n_("dbg1")

        wq_stage = xbuf.ap()

        def stag_wait(slot):
            mk = M.get(("stagst", slot))
            return [mk] if mk else []

        def proj_cols(cc_list, dst, kind):
            for j, cc in enumerate(cc_list):
                slot = j % 2
                st2 = wq_stage[:, slot * 1024:slot * 1024 + 1024].rearrange("p (k c) -> p k c", c=128)
                wv = [("vec", P2LN_V)] + ([("tmm", M[("qk", kind, j - 2)])] if j >= 2 else
                                          ([("tmm", M[("qk", 'k', len(cc_list) + j - 2)])] if (kind == 'q' and j < 2) else []))
                E_("sync", lambda e, cc=cc, st2=st2: e.dma_start(
                    out=st2, in_=wqkv[:, cc * 128:(cc + 1) * 128].rearrange("(k p) c -> p k c", p=128)),
                   waits=wv, inc=f"qs{slot}", incn=16)
                wl = (f"qs{slot}", n_(f"qs{slot}"))
                for tc in range(4):
                    idx = j * 4 + tc
                    bk = bank(idx % 4)
                    wv2 = [wl, ("vec", P2LN_V)]
                    if idx >= 4:
                        wv2.append(("sact", M[("qkev", kind, idx - 4)]))
                    elif kind == 'q':
                        wv2.append(("sact", M[("qkev", 'k', 28 + tc)]))
                        wv2.append(("vec", M["kv_nrm_done"]))
                    first = True
                    for k in range(KC):
                        E_("tensor", lambda e, k=k, tc=tc, bk=bk, st2=st2:
                           e.matmul(bk, st2[:, k], yt[:, k, tc * 512:(tc + 1) * 512], start=(k == 0), stop=(k == KC - 1)),
                           waits=wv2 if first else (), inc="tmm" if k == KC - 1 else None)
                        first = False
                    M[("qk", kind, j)] = n_("tmm")
                    qkt = n_("tmm")
                    sslot = idx % 2
                    stg = stag[:, sslot * 512:(sslot + 1) * 512]
                    bcol = bqkvT[:, cc:cc + 1]
                    E_("scalar", lambda e, bk=bk, stg=stg, bcol=bcol: e.activation(stg, bk, AF.Identity, bias=bcol),
                       waits=[("tmm", qkt)] + stag_wait(sslot), inc="sact")
                    E_("scalar", lambda e, bk=bk, bcol=bcol: e.activation(stagsq[:, :], bk, AF.Square, bias=bcol),
                       waits=[("tmm", M.get(("nrmT", kind, idx - 1), 0))], inc="sact")
                    M[("qkev", kind, idx)] = n_("sact")
                    sa = n_("sact")
                    rowblk = (cc - cc_list[0]) * 128
                    if kind == 'k' or P2_QSTORE:
                        E_("gpsimd", lambda e, rowblk=rowblk, tc=tc, stg=stg, dst=dst:
                           e.dma_start(out=dst[rowblk:rowblk + 128, tc * 512:(tc + 1) * 512], in_=stg),
                           waits=[("sact", sa)], inc=f"st{sslot}", incn=16)
                        M[("stagst", sslot)] = (f"st{sslot}", n_(f"st{sslot}"))
                    for hh in range(2 if (kind == 'k' or P2_QNORM) else 0):
                        h = cc * 2 + hh - (16 if kind == 'k' else 0)
                        nb = bank(4, hh * 512, hh * 512 + 512)
                        E_("tensor", lambda e, hh=hh, nb=nb:
                           e.matmul(nb[0:1, :], ones128[hh * 64:hh * 64 + 64, 0:1], stagsq[hh * 64:hh * 64 + 64, :],
                                    start=True, stop=True),
                           waits=[("sact", sa), ("vec", M.get(("nrmv", kind, idx - 1, hh), 0))], inc="tmm")
                        M[("nrmT", kind, idx)] = n_("tmm")
                        ntm = n_("tmm")
                        if kind == 'k':
                            col = h
                            E_("vector", lambda e, nb=nb: e.tensor_reduce(nrm[0:1, 90:91], nb[0:1, :], axis=AX.X, op=ALU.max),
                               waits=[("tmm", ntm)], inc="vec")
                            if tc == 0:
                                E_("vector", lambda e, col=col: e.tensor_copy(nrm[0:1, col:col + 1], nrm[0:1, 90:91]), inc="vec")
                            else:
                                E_("vector", lambda e, col=col: e.tensor_tensor(nrm[0:1, col:col + 1], nrm[0:1, col:col + 1],
                                                                                nrm[0:1, 90:91], op=ALU.max), inc="vec")
                        else:
                            col = 16 + h * 4 + tc
                            E_("vector", lambda e, nb=nb, col=col: e.tensor_reduce(nrm[0:1, col:col + 1], nb[0:1, :], axis=AX.X, op=ALU.max),
                               waits=[("tmm", ntm)], inc="vec")
                        M[("nrmv", kind, idx, hh)] = n_("vec")

        proj_cols(list(range(8, 16)) if P2_N >= 2 else [], agk_src, 'k')
        K_V = n_("vec")
        M["kv_nrm_done"] = K_V
        if P2_N >= 3:
            E_("gpsimd", lambda e: e.collective_compute("AllGather", ALU.bypass, replica_groups=RG,
                                                        ins=[agk_src], outs=[agk_dst]),
               waits=[("st0", n_("st0")), ("st1", n_("st1"))], inc="cck")
        if P2_N >= 3:
            E_("gpsimd", lambda e: e.dma_start(out=agn_src, in_=nrm[0:1, 0:16]), waits=[("vec", K_V)], inc="stn", incn=16)
            E_("gpsimd", lambda e: e.collective_compute("AllGather", ALU.bypass, replica_groups=RG,
                                                        ins=[agn_src], outs=[agn_dst]),
               waits=[("stn", n_("stn"))], inc="ccn")

        wv_stage = xbuf.ap()[:, 2048:2048 + 8192].rearrange("p (k o) -> p k o", o=1024)
        if P2_N >= 4:
            E_("sync", lambda e: e.dma_start(out=wv_stage, in_=wqkv[:, 2048:3072].rearrange("(k p) o -> p k o", p=128)),
               waits=[("vec", P2LN_V)], inc="wvl", incn=16)
        WVL = n_("wvl")
        acc = psum[:, 5 * 512:7 * 512]
        for i in range(NT if P2_N >= 4 else 0):
            wv = [("wvl", WVL), ("vec", K_V)]
            if i >= 1:
                wv.append(("sact", M[("vev", i - 1)]))
            first = True
            for nn2 in range(2):
                for k in range(KC):
                    last = (nn2 == 1 and k == KC - 1)
                    E_("tensor", lambda e, k=k, nn2=nn2, i=i:
                       e.matmul(acc[:, nn2 * 512:(nn2 + 1) * 512], yt[:, k, i * 128:(i + 1) * 128],
                                wv_stage[:, k, nn2 * 512:(nn2 + 1) * 512], start=(k == 0), stop=(k == KC - 1)),
                       waits=wv if first else (), inc="tmm" if last else None)
                    first = False
            vt = n_("tmm")
            svw = [("tmm", vt)]
            if i >= 1:
                svw.append(("stv", M[("vstag", i - 1)]))
            else:
                svw += stag_wait(0) + stag_wait(1)
            E_("scalar", lambda e: e.activation(stag[:, :], acc, AF.Copy), waits=svw, inc="sact")
            M[("vev", i)] = n_("sact")
            sa = n_("sact")
            E_("gpsimd", lambda e, i=i: e.dma_start(out=agv_src[i * 128:(i + 1) * 128, :], in_=stag[:, 0:1024]),
               waits=[("sact", sa)], inc="stv", incn=16)
            M[("vstag", i)] = n_("stv")
        M[("stagst", 0)] = M[("stagst", 1)] = ("stv", n_("stv"))
        if P2_N >= 5:
            E_("gpsimd", lambda e: e.collective_compute("AllGather", ALU.bypass, replica_groups=RG,
                                                        ins=[agv_src], outs=[agv_dst]),
               waits=[("stv", n_("stv"))], inc="ccv")

        proj_cols(list(range(0, 8)) if P2_N >= 6 else [], qsc, 'q')
        Q_ST_W = [("st0", n_("st0")), ("st1", n_("st1"))]
        P2_T, P2_V, P2_SA = n_("tmm"), n_("vec"), n_("sact")

    P3_T = P3_V = P3_ST = P3_SA = 0
    if PHASE_LIMIT >= 3:
        KRELs = [ybuf[:, 0:4096], ybuf[:, 4096:8192]]
        Vhalf = [ybuf[:, 8192 + hh * 1040: 8192 + (hh + 1) * 1040] for hh in range(2)]
        QQs = [ybuf[:, 10272:12320], ybuf[:, 10272:12320]]
        expT = [ybuf[:, 12320 + q * 512: 12320 + (q + 1) * 512] for q in range(4)]
        attnT = xbuf

        for p in range(2 if P3_SETUP_N >= 1 else 0):
            E_("sync", lambda e, p=p: e.dma_start(out=KRELs[p][64:128, :], in_=relpt),
               waits=[("tmm", P2_T)], inc="rel", incn=16)
        for hh in range(2 if P3_SETUP_N >= 2 else 0):
            vv_ = Vhalf[hh].rearrange("p (c s) -> p c s", s=65)
            E_("vector", lambda e, vv_=vv_: e.memset(vv_[:, :, 64:65], 1.0),
               waits=[("tmm", P2_T)], inc="vec")
        REL_KVQ = n_("rel")
        ONES2_V = n_("vec")

        for c8 in range(8 if P3_SETUP_N >= 3 else 0):
            E_("scalar", lambda e, c8=c8: e.activation(stagsq[64:128, :], KRELs[0][64:128, c8 * 512:(c8 + 1) * 512], AF.Square),
               waits=[("rel", REL_KVQ), ("tmm", M.get(("relT", c8 - 1), P2_T))], inc="sact")
            sa = n_("sact")
            nb = bank(4, 0, 512)
            E_("tensor", lambda e, nb=nb: e.matmul(nb[0:1, :], ones128[64:128, 0:1], stagsq[64:128, :], start=True, stop=True),
               waits=[("sact", sa), ("vec", M.get(("relv", c8 - 1), ONES2_V)), ("tmm", P2_T)], inc="tmm")
            M[("relT", c8)] = n_("tmm")
            tn = n_("tmm")
            E_("vector", lambda e, nb=nb: e.tensor_reduce(nrm[0:1, 90:91], nb[0:1, :], axis=AX.X, op=ALU.max),
               waits=[("tmm", tn)], inc="vec")
            if c8 == 0:
                E_("vector", lambda e: e.tensor_copy(nrm[0:1, 91:92], nrm[0:1, 90:91]), inc="vec")
            else:
                E_("vector", lambda e: e.tensor_tensor(nrm[0:1, 91:92], nrm[0:1, 91:92], nrm[0:1, 90:91], op=ALU.max), inc="vec")
            M[("relv", c8)] = n_("vec")
        if P3_SETUP_N < 4:
            E_("vector", lambda e: e.memset(negc[:, :], -1.0), waits=[("tmm", P2_T)], inc="vec")
            NEGC_V = n_("vec")
        if P3_SETUP_N >= 4:
            E_("sync", lambda e: e.dma_start(out=nrm[0:1, 96:112], in_=agn_dst[0:1, :]), waits=[("ccn", 1)], inc="nrmld", incn=16)
            E_("sync", lambda e: e.dma_start(out=nrm[0:1, 112:128], in_=agn_dst[1:2, :]), waits=[("ccn", 1)], inc="nrmld", incn=16)
            NRM_KVQ = n_("nrmld")
            E_("vector", lambda e: e.tensor_tensor(nrm[0:1, 96:112], nrm[0:1, 96:112], nrm[0:1, 112:128], op=ALU.max),
               waits=[("nrmld", NRM_KVQ), ("vec", n_("vec"))], inc="vec")
            E_("scalar", lambda e: e.activation(nrm[0:1, 96:112], nrm[0:1, 96:112], AF.Sqrt), waits=[("vec", n_("vec"))], inc="sact")
            E_("scalar", lambda e: e.activation(nrm[0:1, 91:92], nrm[0:1, 91:92], AF.Sqrt), inc="sact")
            E_("scalar", lambda e: e.activation(nrm[0:1, 16:80], nrm[0:1, 16:80], AF.Sqrt), waits=[("vec", P2_V)], inc="sact")
            nsq = n_("sact")
            E_("vector", lambda e: e.tensor_scalar(nrm[0:1, 96:112], nrm[0:1, 96:112], nrm[0:1, 91:92], None, op0=ALU.add),
               waits=[("sact", nsq)], inc="vec")
            for h in range(NHEAD):
                E_("vector", lambda e, h=h: e.tensor_scalar(nrm[0:1, 16 + 4 * h:16 + 4 * h + 4], nrm[0:1, 16 + 4 * h:16 + 4 * h + 4],
                                                            nrm[0:1, 96 + h:97 + h], None, op0=ALU.mult), inc="vec")
            E_("vector", lambda e: e.tensor_scalar(nrm[0:1, 16:80], nrm[0:1, 16:80], -0.125, None, op0=ALU.mult), inc="vec")
            ncv = n_("vec")
            nb5 = bank(5, 0, 64)
            E_("tensor", lambda e: e.matmul(nb5, ones1[0:1, :], nrm[0:1, 16:80], start=True, stop=True),
               waits=[("vec", ncv), ("tmm", P2_T)], inc="tmm")
            E_("vector", lambda e: e.tensor_copy(negc[:, :], nb5), waits=[("tmm", n_("tmm"))], inc="vec")
            NEGC_V = n_("vec")

        for h in range(NHEADS_RUN):
            par = h % 2
            KREL, QQ = KRELs[par], QQs[par]
            prev_sc = M.get(("sc_last", h - 2), 0)
            ksem = f"kv{par}"
            E_("sync", lambda e, h=h, KREL=KREL: e.dma_start(out=KREL[0:64, 0:2048], in_=agk_dst[h * 64:(h + 1) * 64, :]),
               waits=[("tmm", prev_sc), ("cck", 1), ("rel", REL_KVQ)], inc=ksem, incn=16)
            E_("sync", lambda e, h=h, KREL=KREL: e.dma_start(out=KREL[0:64, 2048:4096],
                                                             in_=agk_dst[1024 + h * 64: 1024 + (h + 1) * 64, :]),
               waits=[("cck", 1)], inc=ksem, incn=16)
            E_("sync", lambda e, h=h, QQ=QQ: e.dma_start(out=QQ[0:64, :], in_=qsc[h * 64:(h + 1) * 64, :]),
               waits=[("tmm", M.get(("sc_last", h - 1), 0))] + Q_ST_W, inc=ksem, incn=16)
            E_("sync", lambda e, h=h, QQ=QQ: e.dma_start(out=QQ[64:128, :], in_=qsc[h * 64:(h + 1) * 64, :]),
               inc=ksem, incn=16)
            prev_av = M.get(("av_last", h - 1), 0)
            for hh in range(2):
                vv_ = Vhalf[hh].rearrange("p (c s) -> p c s", s=65)
                vsrc = agv_dst[hh * 2048:(hh + 1) * 2048, h * 64:(h + 1) * 64].rearrange("(c p) s -> p c s", p=128)
                E_("sync", lambda e, vv_=vv_, vsrc=vsrc: e.dma_start(out=vv_[:, :, 0:64], in_=vsrc),
                   waits=[("tmm", prev_av), ("ccv", 1), ("vec", ONES2_V)], inc=ksem, incn=16)
            kvq_h = (ksem, n_(ksem))
            for tc in range(4):
                avb = psum[:, 4 * 512:5 * 512]
                prev_ev = M.get(("avev", h, tc - 1)) or M.get(("avev", h - 1, 3), 0)
                prev_rc = M.get(("avrc", h, tc - 1)) or M.get(("avrc", h - 1, 3), 0)
                for kc in range(32):
                    sb = bank(kc % 4)
                    et = expT[kc % 4]
                    wv = [kvq_h, ("vec", NEGC_V)]
                    pe = M.get(("exp", h, tc, kc - 4)) or M.get(("exp", h, tc - 1, 28 + kc)) or M.get(("exp", h - 1, 3, 28 + kc), 0)
                    if pe:
                        wv.append(("sact", pe))
                    E_("tensor", lambda e, sb=sb, kc=kc, tc=tc, KREL=KREL, QQ=QQ:
                       e.matmul(sb, KREL[:, kc * 128:(kc + 1) * 128], QQ[:, tc * 512:(tc + 1) * 512], start=True, stop=True),
                       waits=wv, inc="tmm")
                    M[("sc_last", h)] = n_("tmm")
                    sct = n_("tmm")
                    wv2 = [("tmm", sct)]
                    pavm = M.get(("avmm", h, tc, kc - 4))
                    if pavm:
                        wv2.append(("tmm", pavm))
                    ncol = negc[:, h * 4 + tc: h * 4 + tc + 1]
                    E_("scalar", lambda e, et=et, sb=sb, ncol=ncol: e.activation(et, sb, AF.Exp, bias=ncol, scale=0.125),
                       waits=wv2, inc="sact")
                    M[("exp", h, tc, kc)] = n_("sact")
                    expc = n_("sact")
                    if P3_DO_AV:
                        wv3 = [("sact", expc)]
                        if kc == 0:
                            wv3 += [("vec", prev_ev), ("vec", prev_rc)]
                        vsl = Vhalf[kc // 16][:, (kc % 16) * 65:(kc % 16) * 65 + 65]
                        E_("tensor", lambda e, vsl=vsl, et=et, kc=kc:
                           e.matmul(avb[0:65, :], vsl, et, start=(kc == 0), stop=(kc == 31)),
                           waits=wv3, inc="tmm")
                        M[("avmm", h, tc, kc)] = n_("tmm")
                M[("av_last", h)] = n_("tmm")
                avd = n_("tmm")
                if not (P3_DO_AV and P3_DO_EVAC):
                    M[("avev", h, tc)] = n_("vec")
                    M[("avrc", h, tc)] = n_("vec")
                    M[("avbc", h, tc)] = n_("vec")
                    continue
                E_("vector", lambda e: e.reciprocal(stagsq[64:65, :], avb[64:65, :]),
                   waits=[("tmm", avd), ("tmm", M.get(("avbcT", h, tc - 1)) or M.get(("avbcT", h - 1, 3), 0))], inc="vec")
                M[("avrc", h, tc)] = n_("vec")
                rc = n_("vec")
                bc = bank(5, 0, 512)
                E_("tensor", lambda e, bc=bc: e.matmul(bc[0:64, :], onesr[64:65, 0:64], stagsq[64:65, :], start=True, stop=True),
                   waits=[("vec", rc), ("vec", M.get(("avbc", h, tc - 1)) or M.get(("avbc", h - 1, 3), NEGC_V))],
                   inc="tmm")
                M[("avbcT", h, tc)] = n_("tmm")
                bct = n_("tmm")
                hp = h // 2
                dstcols = slice(hp * 2048 + tc * 512, hp * 2048 + (tc + 1) * 512)
                bvcol = bvT[0:64, h:h + 1]
                if par == 0:
                    E_("vector", lambda e: e.tensor_copy(stgo[0:64, :], avb[0:64, :]),
                       waits=[("tmm", bct), ("sta", M.get(("avst", h - 1, 3), 0))], inc="vec")
                    E_("vector", lambda e, bc=bc: e.tensor_mul(stgo[0:64, :], stgo[0:64, :], bc[0:64, :]), inc="vec")
                    E_("vector", lambda e, dstcols=dstcols, bvcol=bvcol:
                       e.tensor_scalar(attnT[0:64, dstcols], stgo[0:64, :], bvcol, None, op0=ALU.add), inc="vec")
                else:
                    pst = M.get(("avst", h, tc - 1)) or M.get(("avst", h - 2, 3), 0)
                    E_("vector", lambda e: e.tensor_copy(stgo[0:64, :], avb[0:64, :]),
                       waits=[("tmm", bct), ("sta", pst)], inc="vec")
                    E_("vector", lambda e, bc=bc: e.tensor_mul(stgo[0:64, :], stgo[0:64, :], bc[0:64, :]), inc="vec")
                    E_("vector", lambda e, bvcol=bvcol: e.tensor_scalar(stgo[0:64, :], stgo[0:64, :], bvcol, None, op0=ALU.add), inc="vec")
                M[("avbc", h, tc)] = n_("vec")
                M[("avev", h, tc)] = n_("vec")
                if par == 1:
                    ve = n_("vec")
                    E_("gpsimd", lambda e, dstcols=dstcols: e.dma_start(out=attnT[64:128, dstcols], in_=stgo[0:64, :]),
                       waits=[("vec", ve)], inc="sta", incn=16)
                    M[("avst", h, tc)] = n_("sta")
        P3_T, P3_V, P3_ST, P3_SA = n_("tmm"), n_("vec"), n_("sta"), n_("sact")
        if DEBUG:
            E_("gpsimd", lambda e: e.dma_start(out=dbg["dbg_attnT"].rearrange("(k p) t -> p k t", p=128),
                                               in_=attnT.ap().rearrange("p (k t) -> p k t", t=2048)),
               waits=[("vec", P3_V), ("sta", P3_ST)], inc="dbg2", incn=16)

    P4_V = 0
    if PHASE_LIMIT >= 4:
        woutsb = ybuf.ap()[:, 0:8192].rearrange("p (k o) -> p k o", o=1024)
        E_("sync", lambda e: e.dma_start(out=woutsb, in_=wout_d.rearrange("(k p) o -> p k o", p=128)),
           waits=[("tmm", P3_T)], inc="wol", incn=16)
        E_("sync", lambda e: e.dma_start(out=boutrow, in_=bout),
           waits=[("tmm", P3_T)] + Q_ST_W, inc="wol", incn=16)
        WOL = n_("wol")
        for i in range(NT):
            bp = 2 * (i % 2)
            acc = psum[:, bp * 512:(bp + 2) * 512]
            wv = [("wol", WOL), ("vec", P3_V), ("sta", P3_ST), ("sact", P3_SA), ("dbg1", M.get("dbg1", 0))]
            if i >= 2:
                wv.append(("vec", M[("opv", i - 2)]))
            first = True
            for hp in range(8):
                for nn2 in range(2):
                    E_("tensor", lambda e, hp=hp, nn2=nn2, i=i, acc=acc:
                       e.matmul(acc[:, nn2 * 512:(nn2 + 1) * 512],
                                attnT[:, hp * 2048 + i * 128: hp * 2048 + (i + 1) * 128],
                                woutsb[:, hp, nn2 * 512:(nn2 + 1) * 512], start=(hp == 0), stop=False),
                       waits=wv if first else (), inc=None)
                    first = False
            for nn2 in range(2):
                E_("tensor", lambda e, nn2=nn2, acc=acc: e.matmul(acc[:, nn2 * 512:(nn2 + 1) * 512], ones1[0:1, :],
                                                                  boutrow[0:1, nn2 * 512:(nn2 + 1) * 512], start=False, stop=True),
                   inc="tmm" if nn2 == 1 else None)
            opt = n_("tmm")
            xi = wbuf[:, i * 1024:(i + 1) * 1024]
            E_("vector", lambda e, xi=xi, acc=acc: e.tensor_add(xi, xi, acc), waits=[("tmm", opt)], inc="vec")
            M[("opv", i)] = n_("vec")
        P4_V = n_("vec")
        if DEBUG:
            E_("gpsimd", lambda e: e.dma_start(out=dbg["dbg_x2"].rearrange("(i p) c -> p i c", p=128),
                                               in_=wbuf.ap().rearrange("p (i c) -> p i c", c=1024)),
               waits=[("vec", P4_V)], inc="dbg3", incn=16)

    if PHASE_LIMIT >= 5:
        ffn_phase(wbuf, xbuf, ybuf,
                  [("vec", P4_V), ("tmm", n_("tmm")), ("sact", n_("sact")),
                   ("st0", n_("st0")), ("st1", n_("st1")), ("stv", n_("stv")), ("sta", n_("sta")),
                   ("dbg2", n_("dbg2")), ("dbg3", n_("dbg3"))])
    P_LAST_V = n_("vec")

    src = wbuf if PHASE_LIMIT >= 4 else xbuf
    E_("gpsimd", lambda e: e.dma_start(out=y_out.rearrange("(i p) c -> p i c", p=128),
                                       in_=src.ap().rearrange("p (i c) -> p i c", c=1024)),
       waits=[("vec", P_LAST_V), ("tmm", n_("tmm")), ("sact", n_("sact"))], inc="sty", incn=16)
    FIN = n_("sty")
    E_("gpsimd", lambda e: e.nop(), waits=[("sty", FIN)], inc=None)

    sch.check_runnable()
    semctx = []
    sems = {}
    for s in sorted(sch.semnames):
        hd = nc.semaphore(s)
        sems[s] = hd.__enter__()
        semctx.append(hd)
    sch.realize(nc, sems)
    for hd in reversed(semctx):
        hd.__exit__(None, None, None)
    for hd in reversed(ctxs):
        hd.__exit__(None, None, None)
    return nc


def prepare_in_maps(L0_attn_x, L0_attn_Wqkv, L0_attn_bqkv, L0_attn_Wout, L0_attn_bout,
                    L0_attn_rel_emb, L0_ffn_W, L0_ffn_b, L0_ffn_Wg, L0_ffn_bg):
    f32 = np.float32
    shared = {
        "wqkv": np.ascontiguousarray(L0_attn_Wqkv, dtype=f32),
        "bqkv": np.ascontiguousarray(L0_attn_bqkv, dtype=f32).reshape(1, 3 * D),
        "wout": np.ascontiguousarray(L0_attn_Wout, dtype=f32),
        "bout": np.ascontiguousarray(L0_attn_bout, dtype=f32).reshape(1, D),
        "relpt": np.ascontiguousarray(
            np.concatenate([np.zeros((1, DH), f32), np.asarray(L0_attn_rel_emb, f32)], axis=0).T),
        "wt": np.ascontiguousarray(np.asarray(L0_ffn_W, f32).transpose(0, 2, 1)),
        "bffn": np.ascontiguousarray(L0_ffn_b, dtype=f32),
        "wgt": np.ascontiguousarray(np.asarray(L0_ffn_Wg, f32).T),
        "bg": np.ascontiguousarray(L0_ffn_bg, dtype=f32).reshape(1, E),
        "ident": np.eye(128, dtype=f32),
    }
    x = np.asarray(L0_attn_x, f32)
    in_maps = []
    for c in range(8):
        b, hf = c // 2, c % 2
        m = dict(shared)
        m["x"] = np.ascontiguousarray(x[b, hf * TPC:(hf + 1) * TPC, :])
        in_maps.append(m)
    return in_maps


def kernel(L0_attn_x, L0_attn_Wqkv, L0_attn_bqkv, L0_attn_Wout, L0_attn_bout,
           L0_attn_rel_emb, L0_ffn_W, L0_ffn_b, L0_ffn_Wg, L0_ffn_bg):
    in_maps = prepare_in_maps(L0_attn_x, L0_attn_Wqkv, L0_attn_bqkv, L0_attn_Wout, L0_attn_bout,
                              L0_attn_rel_emb, L0_ffn_W, L0_ffn_b, L0_ffn_Wg, L0_ffn_bg)
    nc = build_nc()
    res = run_bass_kernel_spmd(nc, in_maps, core_ids=list(range(8)))
    out = np.empty((B, S, D), np.float32)
    for c in range(8):
        b, hf = c // 2, c % 2
        out[b, hf * TPC:(hf + 1) * TPC, :] = res.results[c]["y"]
    return out


# version 03
# speedup vs baseline: 1.0201x; 1.0201x over previous
import numpy as np
import concourse.bass as bass
import concourse.mybir as mybir
from concourse.bass_utils import run_bass_kernel_spmd

F32 = mybir.dt.float32
BF16 = mybir.dt.bfloat16
AF = mybir.ActivationFunctionType
ALU = mybir.AluOpType
AX = mybir.AxisListType

B, S, D = 4, 4096, 1024
H_, DH, E = 16, 64, 8
TPC = 2048
NT = TPC // 128
KC = D // 128
NHEAD = 16

DEBUG = 0
PHASE_LIMIT = 9
NHEADS_RUN = NHEAD
P2_N = 6
P2_QNORM = 1
P2_QSTORE = 1
P3_DO_AV = 1
P3_DO_EVAC = 1
P3_SETUP_N = 3


class Sched:
    ENGINES = ("sync", "tensor", "vector", "scalar", "gpsimd")

    def __init__(self):
        self.ops = {e: [] for e in self.ENGINES}
        self.ctr = {}
        self.semnames = set()
        self.marks = {}

    def n(self, sem):
        return self.ctr.get(sem, 0)

    AUTOCHAIN = {"vector": "vec", "scalar": "sact"}

    def emit(self, eng, fn, waits=(), inc=None, incn=1):
        w = [(s, v) for (s, v) in waits if v and v > 0]
        if eng in self.AUTOCHAIN and inc is None:
            inc = self.AUTOCHAIN[eng]
            incn = 1
        self.ops[eng].append((w, fn, inc, incn))
        if inc is not None:
            self.semnames.add(inc)
            self.ctr[inc] = self.ctr.get(inc, 0) + incn
        for s, _ in w:
            self.semnames.add(s)

    def check_runnable(self):
        idx = {e: 0 for e in self.ops}
        cur = {s: 0 for s in self.semnames}
        total = sum(len(v) for v in self.ops.values())
        done = 0
        progress = True
        while progress:
            progress = False
            for e in self.ops:
                while idx[e] < len(self.ops[e]):
                    w, _, inc, incn = self.ops[e][idx[e]]
                    if any(cur.get(s, 0) < v for s, v in w):
                        break
                    if inc is not None:
                        cur[inc] = cur.get(inc, 0) + incn
                    idx[e] += 1
                    done += 1
                    progress = True
        if done != total:
            stuck = {e: (len(self.ops[e]) - idx[e]) for e in self.ops if idx[e] < len(self.ops[e])}
            first = {e: self.ops[e][idx[e]][0] for e in stuck}
            raise RuntimeError(f"schedule deadlock: remaining={stuck} first_waits={first}")

    def realize(self, nc, sems):
        ctx = nc.Block()
        block = ctx.__enter__()
        try:
            for eng in self.ENGINES:
                ops = self.ops[eng]
                if not ops:
                    continue

                chain = Sched.AUTOCHAIN.get(eng)

                def section(engine, ops=ops, chain=chain):
                    hwm = {}
                    idx = 0
                    for w, fn, inc, incn in ops:
                        if chain is not None and idx > 0:
                            w = list(w) + [(chain, idx)]
                        idx += 1
                        for s, v in w:
                            if hwm.get(s, -1) < v:
                                engine.wait_ge(sems[s], v)
                                hwm[s] = v
                        ins = fn(engine)
                        if inc is not None:
                            ins.then_inc(sems[inc], incn)
                getattr(block, eng)(section)
        finally:
            ctx.__exit__(None, None, None)


def build_nc():
    nc = bass.Bass("TRN2", target_bir_lowering=False, dynamic_dma_scratch_size=16384)

    x_in = nc.dram_tensor("x", [TPC, D], F32, kind="ExternalInput").ap()
    wqkv = nc.dram_tensor("wqkv", [D, 3 * D], F32, kind="ExternalInput").ap()
    bqkv = nc.dram_tensor("bqkv", [1, 3 * D], F32, kind="ExternalInput").ap()
    wout_d = nc.dram_tensor("wout", [D, D], F32, kind="ExternalInput").ap()
    bout = nc.dram_tensor("bout", [1, D], F32, kind="ExternalInput").ap()
    relpt = nc.dram_tensor("relpt", [DH, S], F32, kind="ExternalInput").ap()
    wt_d = nc.dram_tensor("wt", [E, D, D], F32, kind="ExternalInput").ap()
    bffn_d = nc.dram_tensor("bffn", [E, D], F32, kind="ExternalInput").ap()
    wgt_d = nc.dram_tensor("wgt", [D, E], F32, kind="ExternalInput").ap()
    bg_d = nc.dram_tensor("bg", [1, E], F32, kind="ExternalInput").ap()
    ident_d = nc.dram_tensor("ident", [128, 128], F32, kind="ExternalInput").ap()
    y_out = nc.dram_tensor("y", [TPC, D], F32, kind="ExternalOutput").ap()
    dbg = {}
    if DEBUG:
        for nm in ("dbg_x1", "dbg_x2"):
            dbg[nm] = nc.dram_tensor(nm, [TPC, D], F32, kind="ExternalOutput").ap()
        dbg["dbg_attnT"] = nc.dram_tensor("dbg_attnT", [D, TPC], F32, kind="ExternalOutput").ap()

    agk_src = nc.dram_tensor("agk_src", [D, TPC], F32).ap()
    agk_dst = nc.dram_tensor("agk_dst", [2 * D, TPC], F32).ap()
    agv_src = nc.dram_tensor("agv_src", [TPC, D], F32).ap()
    agv_dst = nc.dram_tensor("agv_dst", [S, D], F32).ap()
    agn_src = nc.dram_tensor("agn_src", [1, NHEAD], F32).ap()
    agn_dst = nc.dram_tensor("agn_dst", [2, NHEAD], F32).ap()
    qsc = nc.dram_tensor("qsc", [D, TPC], F32).ap()

    sch = Sched()
    RG = [[0, 1], [2, 3], [4, 5], [6, 7]]

    ctxs = []

    def alloc_sb(name, shape, dtype=F32):
        h = nc.sbuf_tensor(name, shape, dtype)
        t = h.__enter__()
        ctxs.append(h)
        return t

    def alloc_ps(name, shape):
        h = nc.psum_tensor(name, shape)
        t = h.__enter__()
        ctxs.append(h)
        return t

    xbuf = alloc_sb("xbuf", [128, 16384])
    ybuf = alloc_sb("ybuf", [128, 16384])
    wbuf = alloc_sb("wbuf", [128, 16384])
    yscr = alloc_sb("yscr", [128, 1024])
    stag = yscr
    stagsq = alloc_sb("stagsq", [128, 512])
    stgo = stagsq
    ident = alloc_sb("identt", [128, 128])
    ones128 = alloc_sb("ones128", [128, 1])
    ones1 = alloc_sb("ones1", [1, 128])
    bqkvT = alloc_sb("bqkvT", [128, 16])
    bvT = alloc_sb("bvT", [64, NHEAD])
    boutrow = yscr[0:1, 0:1024]
    bgrow = alloc_sb("bgrow", [1, E])
    bffn = alloc_sb("bffnsb", [E, D])
    wgt = alloc_sb("wgtsb", [128, KC * E])
    wgthi = alloc_sb("wgthi", [128, KC * E], BF16)
    wgtlo = alloc_sb("wgtlo", [128, KC * E], BF16)
    bghi = alloc_sb("bghi", [1, E], BF16)
    bglo = alloc_sb("bglo", [1, E], BF16)
    ones1b = alloc_sb("ones1b", [1, 128], BF16)
    gate = alloc_sb("gatesb", [128, 2 * E])
    whalf = alloc_sb("whalfsb", [128, NT * E])
    whalfT = stagsq[0:8, 0:128]
    keep = alloc_sb("keepm", [128, E])
    lnst = alloc_sb("lnst", [128, 8])
    epsv = alloc_sb("epsv", [128, 1])
    nrm = alloc_sb("nrm", [1, 256])
    negc = alloc_sb("negc", [128, 64])
    onesr = alloc_sb("onesr", [128, 64])

    psum = alloc_ps("ps", [128, 4096])

    def bank(i, lo=0, hi=512):
        return psum[:, i * 512 + lo: i * 512 + hi]

    E_ = sch.emit
    n_ = sch.n
    M = sch.marks

    x_view = x_in.rearrange("(i p) c -> p i c", p=128)
    xb_t = xbuf.ap().rearrange("p (i c) -> p i c", c=1024)
    E_("sync", lambda e: e.dma_start(out=xb_t, in_=x_view), inc="ldx", incn=16)
    E_("sync", lambda e: e.dma_start(out=ident[:, :], in_=ident_d), inc="ldc", incn=16)
    E_("sync", lambda e: e.dma_start(out=bqkvT[:, :], in_=bqkv[0:1, 0:2048].rearrange("o (c p) -> (o p) c", p=128), allow_slow_non_contiguous=True), inc="ldc", incn=16)
    E_("sync", lambda e: e.dma_start(out=bvT[:, :], in_=bqkv[0:1, 2048:3072].rearrange("o (h p) -> (o p) h", p=64), allow_slow_non_contiguous=True), inc="ldc", incn=16)
    E_("sync", lambda e: e.dma_start(out=bgrow[0:1, :], in_=bg_d), inc="ldc", incn=16)
    E_("sync", lambda e: e.dma_start(out=bffn[:, :], in_=bffn_d), inc="ldc", incn=16)
    E_("sync", lambda e: e.dma_start(out=wgt[:, :].rearrange("p (k e) -> p k e", e=E), in_=wgt_d.rearrange("(k p) e -> p k e", p=128)), inc="ldc", incn=16)
    E_("vector", lambda e: e.memset(ones128[:, :], 1.0), inc="vec")
    E_("vector", lambda e: e.memset(ones1[0:1, :], 1.0), inc="vec")
    E_("vector", lambda e: e.memset(epsv[:, :], 1e-5), inc="vec")
    if PHASE_LIMIT >= 3:
        E_("vector", lambda e: e.memset(onesr[:, :], 1.0), inc="vec")
    E_("vector", lambda e: e.memset(ones1b[0:1, :], 1.0), inc="vec")
    LD_ALL = n_("ldc")
    E_("vector", lambda e: e.tensor_copy(wgthi[:, :], wgt[:, :]), waits=[("ldc", LD_ALL)], inc="vec")
    E_("vector", lambda e: e.scalar_tensor_tensor(wgtlo[:, :], wgthi[:, :], -1.0, wgt[:, :], op0=ALU.mult, op1=ALU.add), inc="vec")
    E_("vector", lambda e: e.tensor_copy(bghi[0:1, :], bgrow[0:1, :]), waits=[("ldc", LD_ALL)], inc="vec")
    E_("vector", lambda e: e.scalar_tensor_tensor(bglo[0:1, :], bghi[0:1, :], -1.0, bgrow[0:1, :], op0=ALU.mult, op1=ALU.add), inc="vec")

    def ln_tile(xi, waits_first):
        sums, sumsq, mean, ex2, negv, rstd = (lnst[:, j:j + 1] for j in range(6))
        w0 = list(waits_first) + [("sact", n_("sact")), ("tmm", n_("tmm"))]
        E_("vector", lambda e, xi=xi: e.reduce_sum(sums, xi, axis=AX.X), waits=w0, inc="vec")
        E_("scalar", lambda e, xi=xi: e.activation(yscr[:, :], xi, AF.Square, accum_out=sumsq),
           waits=list(waits_first) + [("tmm", n_("tmm")), ("vec", n_("vec") - 1)], inc="sact")
        swt = n_("sact")
        E_("vector", lambda e: e.tensor_scalar(mean, sums, 1.0 / 1024.0, None, op0=ALU.mult), inc="vec")
        E_("vector", lambda e: e.tensor_scalar(ex2, sumsq, 1.0 / 1024.0, None, op0=ALU.mult),
           waits=[("sact", swt)], inc="vec")
        E_("vector", lambda e: e.scalar_tensor_tensor(negv, mean, mean[:, 0:1], ex2, op0=ALU.mult, op1=ALU.subtract), inc="vec")
        E_("scalar", lambda e: e.activation(rstd, negv, AF.Sqrt, bias=epsv[:, 0:1], scale=-1.0),
           waits=[("vec", n_("vec"))], inc="sact")
        E_("vector", lambda e: e.reciprocal(negv, rstd), waits=[("sact", n_("sact"))], inc="vec")
        E_("vector", lambda e, xi=xi: e.tensor_scalar(yscr[:, :], xi, mean[:, 0:1], negv[:, 0:1], op0=ALU.subtract, op1=ALU.mult),
           inc="vec")
        yv = n_("vec")
        tw = [("vec", yv), ("ldc", LD_ALL), ("vec", M.get("ytcopy", 0))]
        for c in range(KC):
            bk = bank(0) if c < 4 else bank(1)
            sub = bk[:, (c % 4) * 128:(c % 4) * 128 + 128]
            E_("tensor", lambda e, s=sub, c=c: e.transpose(s, yscr[:, c * 128:(c + 1) * 128], ident[:, :]),
               waits=tw if c == 0 else (), inc="tmm")
        return n_("tmm")

    def yt_copy(yt, i, tt):
        for half in range(2):
            def cp(e, i=i, half=half):
                outap = yt[:, 4 * half:4 * half + 4, i * 128:(i + 1) * 128]
                inap = bank(half)[:, :].rearrange("p (c t) -> p c t", t=128)
                return e.tensor_copy(outap, inap)
            E_("vector", cp, waits=[("tmm", tt)], inc="vec")
        M["ytcopy"] = n_("vec")
        return n_("vec")

    def ffn_phase(xsrc, ytb, wstream, wait_x, presplit=False):
        yth = ytb.ap()[:, 0:8192].bitcast(BF16).rearrange("p (k t) -> p k t", t=TPC)
        ytl = ytb.ap()[:, 8192:16384].bitcast(BF16).rearrange("p (k t) -> p k t", t=TPC)
        wsb = wstream.ap()
        whiv = [wsb[:, p * 8192: p * 8192 + 4096].bitcast(BF16).rearrange("p (k o) -> p k o", o=1024) for p in range(2)]
        wlov = [wsb[:, p * 8192 + 4096: (p + 1) * 8192].bitcast(BF16).rearrange("p (k o) -> p k o", o=1024) for p in range(2)]

        def wsplit(e_i, waits0):
            par = e_i % 2
            sem = f"wf{par}"
            for k in range(KC):
                wv = (list(waits0) + [("vec", M.get("wsp_last", 0))]) if k == 0 else [("vec", M[("wsp", e_i, k - 1)])]
                E_("sync", lambda e, e_i=e_i, k=k: e.dma_start(
                    out=yscr[:, :], in_=wt_d[e_i].rearrange("(k p) o -> p k o", p=128)[:, k]),
                   waits=wv, inc=sem, incn=16)
                wfc = n_(sem)
                E_("vector", lambda e, k=k, par=par: e.tensor_copy(whiv[par][:, k], yscr[:, :]),
                   waits=[(sem, wfc)], inc="vec")
                E_("vector", lambda e, k=k, par=par: e.scalar_tensor_tensor(
                    wlov[par][:, k], whiv[par][:, k], -1.0, yscr[:, :], op0=ALU.mult, op1=ALU.add), inc="vec")
                M[("wsp", e_i, k)] = n_("vec")
            M[("wsplit", e_i)] = n_("vec")
            M["wsp_last"] = n_("vec")

        if presplit:
            wsplit(0, wait_x)
            wsplit(1, wait_x)
        for i in range(NT):
            xi = xsrc[:, i * 1024:(i + 1) * 1024]
            tt = ln_tile(xi, wait_x)
            for half in range(2):
                def cph(e, i=i, half=half):
                    outap = yth[:, 4 * half:4 * half + 4, i * 128:(i + 1) * 128]
                    inap = bank(half)[:, :].rearrange("p (c t) -> p c t", t=128)
                    return e.tensor_copy(outap, inap)
                E_("vector", cph, waits=[("tmm", tt)], inc="vec")
                def cpl(e, i=i, half=half):
                    outap = ytl[:, 4 * half:4 * half + 4, i * 128:(i + 1) * 128]
                    hiap = yth[:, 4 * half:4 * half + 4, i * 128:(i + 1) * 128]
                    inap = bank(half)[:, :].rearrange("p (c t) -> p c t", t=128)
                    return e.scalar_tensor_tensor(outap, hiap, -1.0, inap, op0=ALU.mult, op1=ALU.add)
                E_("vector", cpl, inc="vec")
            M["ytcopy"] = n_("vec")
            ytv = n_("vec")
            gb = bank(2, 0, 8)
            gw = [("vec", ytv), ("ldc", LD_ALL), ("sact", M.get(("gexp", i - 1), 0)),
                  ("vec", M.get(("gred", i - 1), 0)), ("vec", M.get(("biasv", i - 1), 0))]
            for k in range(KC):
                for lh, rh, st2 in ((yth, wgthi, k == 0), (yth, wgtlo, False), (ytl, wgthi, False)):
                    E_("tensor", lambda e, k=k, i=i, lh=lh, rh=rh, st2=st2:
                       e.matmul(gb, lh[:, k, i * 128:(i + 1) * 128],
                                rh[:, k * E:(k + 1) * E], start=st2, stop=False),
                       waits=gw if (k == 0 and st2) else (), inc=None)
            E_("tensor", lambda e: e.matmul(gb, ones1b[0:1, :], bghi[0:1, :], start=False, stop=False), inc=None)
            E_("tensor", lambda e: e.matmul(gb, ones1b[0:1, :], bglo[0:1, :], start=False, stop=True), inc="tmm")
            gt = n_("tmm")
            sums, mean = lnst[:, 0:1], lnst[:, 2:3]
            m1n, p1, p2 = lnst[:, 6:7], lnst[:, 7:8], lnst[:, 6:7]
            eg = gate[:, 0:E]
            pr = gate[:, E:2 * E]
            E_("vector", lambda e: e.reduce_max(m1n, gb, axis=AX.X, negate=True), waits=[("tmm", gt)], inc="vec")
            M[("gred", i)] = n_("vec")
            E_("scalar", lambda e, eg=eg: e.activation(eg, gb, AF.Exp, bias=m1n[:, 0:1], accum_out=sums),
               waits=[("vec", n_("vec")), ("tmm", gt)], inc="sact")
            M[("gexp", i)] = n_("sact")
            E_("vector", lambda e: e.reciprocal(mean, sums), waits=[("sact", n_("sact"))], inc="vec")
            E_("vector", lambda e, eg=eg, pr=pr: e.tensor_scalar(pr, eg, mean[:, 0:1], None, op0=ALU.mult), inc="vec")
            E_("vector", lambda e, pr=pr: e.reduce_max(p1, pr, axis=AX.X), inc="vec")
            E_("vector", lambda e, pr=pr: e.tensor_scalar(keep[:, :], pr, p1[:, 0:1], None, op0=ALU.is_ge), inc="vec")
            E_("vector", lambda e, pr=pr, eg=eg: e.scalar_tensor_tensor(eg, keep[:, :], -3.0, pr, op0=ALU.mult, op1=ALU.add), inc="vec")
            E_("vector", lambda e, eg=eg: e.reduce_max(p2, eg, axis=AX.X), inc="vec")
            E_("vector", lambda e, pr=pr: e.tensor_scalar(keep[:, :], pr, p2[:, 0:1], None, op0=ALU.is_ge), inc="vec")
            wh = whalf[:, i * E:(i + 1) * E]
            E_("vector", lambda e, pr=pr, wh=wh: e.scalar_tensor_tensor(wh, pr, 0.5, keep[:, :], op0=ALU.mult, op1=ALU.mult), inc="vec")
            whv = n_("vec")
            tb = bank(3)[0:8, 0:128]
            E_("tensor", lambda e, wh=wh, tb=tb: e.transpose(tb, wh, ident[:, :]),
               waits=[("vec", whv), ("vec", M.get(("whtc", i - 1), 0)), ("vec", M.get(("biasv", i - 1), 0))], inc="tmm")
            E_("vector", lambda e, tb=tb: e.tensor_copy(whalfT[:, :], tb),
               waits=[("tmm", n_("tmm"))], inc="vec")
            M[("whtc", i)] = n_("vec")
            bacc = psum[:, 2 * 512:4 * 512]
            bw = [("vec", M[("whtc", i)]), ("sact", M[("gexp", i)]), ("ldc", LD_ALL)]
            for nn2 in range(2):
                E_("tensor", lambda e, nn2=nn2, bacc=bacc:
                   e.matmul(bacc[:, nn2 * 512:(nn2 + 1) * 512], whalfT[:, :],
                            bffn[:, nn2 * 512:(nn2 + 1) * 512], start=True, stop=True),
                   waits=bw if nn2 == 0 else (), inc="tmm" if nn2 == 1 else None)
            bt = n_("tmm")
            E_("vector", lambda e, xi=xi, bacc=bacc: e.tensor_add(xi, xi, bacc), waits=[("tmm", bt)], inc="vec")
            M[("biasv", i)] = n_("vec")
        A_T, A_V = n_("tmm"), n_("vec")

        if not presplit:
            wsplit(0, list(wait_x) + [("vec", A_V)])
            wsplit(1, list(wait_x) + [("vec", A_V)])
        for e_i in range(E):
            par = e_i % 2
            for i in range(NT):
                bp = 4 + 2 * (i % 2)
                acc = psum[:, bp * 512:(bp + 2) * 512]
                idx = e_i * NT + i
                wv = [("vec", M[("wsplit", e_i)]),
                      ("vec", A_V if idx < 2 else M[("ffnev", idx - 2)])]
                plan = []
                for k in range(KC):
                    for lh, rh in ((yth, whiv[par]), (yth, wlov[par]), (ytl, whiv[par])):
                        for nn2 in range(2 if rh is not wlov[par] else 2):
                            plan.append((k, lh, rh, nn2))
                nplan = len(plan)
                for j, (k, lh, rh, nn2) in enumerate(plan):
                    E_("tensor", lambda e, k=k, lh=lh, rh=rh, nn2=nn2, i=i, acc=acc, j=j, nplan=nplan:
                       e.matmul(acc[:, nn2 * 512:(nn2 + 1) * 512], lh[:, k, i * 128:(i + 1) * 128],
                                rh[:, k, nn2 * 512:(nn2 + 1) * 512],
                                start=(j <= 1), stop=(j >= nplan - 2)),
                       waits=wv if j == 0 else (), inc="tmm" if j == nplan - 1 else None)
                M[("ffnmm", e_i)] = n_("tmm")
                mmd = n_("tmm")
                xi = xsrc[:, i * 1024:(i + 1) * 1024]
                whcol = whalf[:, i * E + e_i: i * E + e_i + 1]
                E_("vector", lambda e, xi=xi, acc=acc, whcol=whcol:
                   e.scalar_tensor_tensor(xi, acc, whcol, xi, op0=ALU.mult, op1=ALU.add),
                   waits=[("tmm", mmd)], inc="vec")
                M[("ffnev", idx)] = n_("vec")
            if e_i + 2 < E:
                wsplit(e_i + 2, [("tmm", M[("ffnmm", e_i)]), ("vec", M[("wsp", e_i + 1, KC - 1)])])
        return n_("vec")

    P1_V = ffn_phase(xbuf, ybuf, wbuf, [("ldx", 16)], presplit=True)
    P1_T = n_("tmm")

    P2_T = P2_V = P2_SA = Q_ST = 0
    if PHASE_LIMIT >= 2:
        yt = ybuf.ap().rearrange("p (k t) -> p k t", t=TPC)
        for i in range(NT):
            xi = xbuf[:, i * 1024:(i + 1) * 1024]
            tt = ln_tile(xi, [("vec", P1_V), ("tmm", P1_T)])
            yt_copy(yt, i, tt)
            E_("vector", lambda e, i=i, xi=xi: e.tensor_copy(wbuf[:, i * 1024:(i + 1) * 1024], xi), inc="vec")
        P2LN_V = n_("vec")
        if DEBUG:
            E_("gpsimd", lambda e: e.dma_start(out=dbg["dbg_x1"].rearrange("(i p) c -> p i c", p=128),
                                               in_=wbuf.ap().rearrange("p (i c) -> p i c", c=1024)),
               waits=[("vec", P2LN_V)], inc="dbg1", incn=16)
        M["dbg1"] = n_("dbg1")

        wq_stage = xbuf.ap()

        def stag_wait(slot):
            mk = M.get(("stagst", slot))
            return [mk] if mk else []

        def proj_cols(cc_list, dst, kind):
            for j, cc in enumerate(cc_list):
                slot = j % 2
                st2 = wq_stage[:, slot * 1024:slot * 1024 + 1024].rearrange("p (k c) -> p k c", c=128)
                wv = [("vec", P2LN_V)] + ([("tmm", M[("qk", kind, j - 2)])] if j >= 2 else
                                          ([("tmm", M[("qk", 'k', len(cc_list) + j - 2)])] if (kind == 'q' and j < 2) else []))
                E_("sync", lambda e, cc=cc, st2=st2: e.dma_start(
                    out=st2, in_=wqkv[:, cc * 128:(cc + 1) * 128].rearrange("(k p) c -> p k c", p=128)),
                   waits=wv, inc=f"qs{slot}", incn=16)
                wl = (f"qs{slot}", n_(f"qs{slot}"))
                for tc in range(4):
                    idx = j * 4 + tc
                    bk = bank(idx % 4)
                    wv2 = [wl, ("vec", P2LN_V)]
                    if idx >= 4:
                        wv2.append(("sact", M[("qkev", kind, idx - 4)]))
                    elif kind == 'q':
                        wv2.append(("sact", M[("qkev", 'k', 28 + tc)]))
                        wv2.append(("vec", M["kv_nrm_done"]))
                    first = True
                    for k in range(KC):
                        E_("tensor", lambda e, k=k, tc=tc, bk=bk, st2=st2:
                           e.matmul(bk, st2[:, k], yt[:, k, tc * 512:(tc + 1) * 512], start=(k == 0), stop=(k == KC - 1)),
                           waits=wv2 if first else (), inc="tmm" if k == KC - 1 else None)
                        first = False
                    M[("qk", kind, j)] = n_("tmm")
                    qkt = n_("tmm")
                    sslot = idx % 2
                    stg = stag[:, sslot * 512:(sslot + 1) * 512]
                    bcol = bqkvT[:, cc:cc + 1]
                    E_("scalar", lambda e, bk=bk, stg=stg, bcol=bcol: e.activation(stg, bk, AF.Identity, bias=bcol),
                       waits=[("tmm", qkt)] + stag_wait(sslot), inc="sact")
                    E_("scalar", lambda e, bk=bk, bcol=bcol: e.activation(stagsq[:, :], bk, AF.Square, bias=bcol),
                       waits=[("tmm", M.get(("nrmT", kind, idx - 1), 0))], inc="sact")
                    M[("qkev", kind, idx)] = n_("sact")
                    sa = n_("sact")
                    rowblk = (cc - cc_list[0]) * 128
                    if kind == 'k' or P2_QSTORE:
                        E_("gpsimd", lambda e, rowblk=rowblk, tc=tc, stg=stg, dst=dst:
                           e.dma_start(out=dst[rowblk:rowblk + 128, tc * 512:(tc + 1) * 512], in_=stg),
                           waits=[("sact", sa)], inc=f"st{sslot}", incn=16)
                        M[("stagst", sslot)] = (f"st{sslot}", n_(f"st{sslot}"))
                    for hh in range(2 if (kind == 'k' or P2_QNORM) else 0):
                        h = cc * 2 + hh - (16 if kind == 'k' else 0)
                        nb = bank(4, hh * 512, hh * 512 + 512)
                        E_("tensor", lambda e, hh=hh, nb=nb:
                           e.matmul(nb[0:1, :], ones128[hh * 64:hh * 64 + 64, 0:1], stagsq[hh * 64:hh * 64 + 64, :],
                                    start=True, stop=True),
                           waits=[("sact", sa), ("vec", M.get(("nrmv", kind, idx - 1, hh), 0))], inc="tmm")
                        M[("nrmT", kind, idx)] = n_("tmm")
                        ntm = n_("tmm")
                        if kind == 'k':
                            col = h
                            E_("vector", lambda e, nb=nb: e.tensor_reduce(nrm[0:1, 90:91], nb[0:1, :], axis=AX.X, op=ALU.max),
                               waits=[("tmm", ntm)], inc="vec")
                            if tc == 0:
                                E_("vector", lambda e, col=col: e.tensor_copy(nrm[0:1, col:col + 1], nrm[0:1, 90:91]), inc="vec")
                            else:
                                E_("vector", lambda e, col=col: e.tensor_tensor(nrm[0:1, col:col + 1], nrm[0:1, col:col + 1],
                                                                                nrm[0:1, 90:91], op=ALU.max), inc="vec")
                        else:
                            col = 16 + h * 4 + tc
                            E_("vector", lambda e, nb=nb, col=col: e.tensor_reduce(nrm[0:1, col:col + 1], nb[0:1, :], axis=AX.X, op=ALU.max),
                               waits=[("tmm", ntm)], inc="vec")
                        M[("nrmv", kind, idx, hh)] = n_("vec")

        proj_cols(list(range(8, 16)) if P2_N >= 2 else [], agk_src, 'k')
        K_V = n_("vec")
        M["kv_nrm_done"] = K_V
        if P2_N >= 3:
            E_("gpsimd", lambda e: e.collective_compute("AllGather", ALU.bypass, replica_groups=RG,
                                                        ins=[agk_src], outs=[agk_dst]),
               waits=[("st0", n_("st0")), ("st1", n_("st1"))], inc="cck")
        if P2_N >= 3:
            E_("gpsimd", lambda e: e.dma_start(out=agn_src, in_=nrm[0:1, 0:16]), waits=[("vec", K_V)], inc="stn", incn=16)
            E_("gpsimd", lambda e: e.collective_compute("AllGather", ALU.bypass, replica_groups=RG,
                                                        ins=[agn_src], outs=[agn_dst]),
               waits=[("stn", n_("stn"))], inc="ccn")

        wv_stage = xbuf.ap()[:, 2048:2048 + 8192].rearrange("p (k o) -> p k o", o=1024)
        if P2_N >= 4:
            E_("sync", lambda e: e.dma_start(out=wv_stage, in_=wqkv[:, 2048:3072].rearrange("(k p) o -> p k o", p=128)),
               waits=[("vec", P2LN_V)], inc="wvl", incn=16)
        WVL = n_("wvl")
        acc = psum[:, 5 * 512:7 * 512]
        for i in range(NT if P2_N >= 4 else 0):
            wv = [("wvl", WVL), ("vec", K_V)]
            if i >= 1:
                wv.append(("sact", M[("vev", i - 1)]))
            first = True
            for nn2 in range(2):
                for k in range(KC):
                    last = (nn2 == 1 and k == KC - 1)
                    E_("tensor", lambda e, k=k, nn2=nn2, i=i:
                       e.matmul(acc[:, nn2 * 512:(nn2 + 1) * 512], yt[:, k, i * 128:(i + 1) * 128],
                                wv_stage[:, k, nn2 * 512:(nn2 + 1) * 512], start=(k == 0), stop=(k == KC - 1)),
                       waits=wv if first else (), inc="tmm" if last else None)
                    first = False
            vt = n_("tmm")
            svw = [("tmm", vt)]
            if i >= 1:
                svw.append(("stv", M[("vstag", i - 1)]))
            else:
                svw += stag_wait(0) + stag_wait(1)
            E_("scalar", lambda e: e.activation(stag[:, :], acc, AF.Copy), waits=svw, inc="sact")
            M[("vev", i)] = n_("sact")
            sa = n_("sact")
            E_("gpsimd", lambda e, i=i: e.dma_start(out=agv_src[i * 128:(i + 1) * 128, :], in_=stag[:, 0:1024]),
               waits=[("sact", sa)], inc="stv", incn=16)
            M[("vstag", i)] = n_("stv")
        M[("stagst", 0)] = M[("stagst", 1)] = ("stv", n_("stv"))
        if P2_N >= 5:
            E_("gpsimd", lambda e: e.collective_compute("AllGather", ALU.bypass, replica_groups=RG,
                                                        ins=[agv_src], outs=[agv_dst]),
               waits=[("stv", n_("stv"))], inc="ccv")

        proj_cols(list(range(0, 8)) if P2_N >= 6 else [], qsc, 'q')
        Q_ST_W = [("st0", n_("st0")), ("st1", n_("st1"))]
        P2_T, P2_V, P2_SA = n_("tmm"), n_("vec"), n_("sact")

    P3_T = P3_V = P3_ST = P3_SA = 0
    if PHASE_LIMIT >= 3:
        KRELs = [ybuf[:, 0:4096], ybuf[:, 4096:8192]]
        Vhalf = [ybuf[:, 8192 + hh * 1040: 8192 + (hh + 1) * 1040] for hh in range(2)]
        QQs = [ybuf[:, 10272:12320], ybuf[:, 10272:12320]]
        expT = [ybuf[:, 12320 + q * 512: 12320 + (q + 1) * 512] for q in range(4)]
        attnT = xbuf

        for p in range(2 if P3_SETUP_N >= 1 else 0):
            E_("sync", lambda e, p=p: e.dma_start(out=KRELs[p][64:128, :], in_=relpt),
               waits=[("tmm", P2_T)], inc="rel", incn=16)
        for hh in range(2 if P3_SETUP_N >= 2 else 0):
            vv_ = Vhalf[hh].rearrange("p (c s) -> p c s", s=65)
            E_("vector", lambda e, vv_=vv_: e.memset(vv_[:, :, 64:65], 1.0),
               waits=[("tmm", P2_T)], inc="vec")
        REL_KVQ = n_("rel")
        ONES2_V = n_("vec")

        for c8 in range(8 if P3_SETUP_N >= 3 else 0):
            E_("scalar", lambda e, c8=c8: e.activation(stagsq[64:128, :], KRELs[0][64:128, c8 * 512:(c8 + 1) * 512], AF.Square),
               waits=[("rel", REL_KVQ), ("tmm", M.get(("relT", c8 - 1), P2_T))], inc="sact")
            sa = n_("sact")
            nb = bank(4, 0, 512)
            E_("tensor", lambda e, nb=nb: e.matmul(nb[0:1, :], ones128[64:128, 0:1], stagsq[64:128, :], start=True, stop=True),
               waits=[("sact", sa), ("vec", M.get(("relv", c8 - 1), ONES2_V)), ("tmm", P2_T)], inc="tmm")
            M[("relT", c8)] = n_("tmm")
            tn = n_("tmm")
            E_("vector", lambda e, nb=nb: e.tensor_reduce(nrm[0:1, 90:91], nb[0:1, :], axis=AX.X, op=ALU.max),
               waits=[("tmm", tn)], inc="vec")
            if c8 == 0:
                E_("vector", lambda e: e.tensor_copy(nrm[0:1, 91:92], nrm[0:1, 90:91]), inc="vec")
            else:
                E_("vector", lambda e: e.tensor_tensor(nrm[0:1, 91:92], nrm[0:1, 91:92], nrm[0:1, 90:91], op=ALU.max), inc="vec")
            M[("relv", c8)] = n_("vec")
        if P3_SETUP_N < 4:
            E_("vector", lambda e: e.memset(negc[:, :], -1.0), waits=[("tmm", P2_T)], inc="vec")
            NEGC_V = n_("vec")
        if P3_SETUP_N >= 4:
            E_("sync", lambda e: e.dma_start(out=nrm[0:1, 96:112], in_=agn_dst[0:1, :]), waits=[("ccn", 1)], inc="nrmld", incn=16)
            E_("sync", lambda e: e.dma_start(out=nrm[0:1, 112:128], in_=agn_dst[1:2, :]), waits=[("ccn", 1)], inc="nrmld", incn=16)
            NRM_KVQ = n_("nrmld")
            E_("vector", lambda e: e.tensor_tensor(nrm[0:1, 96:112], nrm[0:1, 96:112], nrm[0:1, 112:128], op=ALU.max),
               waits=[("nrmld", NRM_KVQ), ("vec", n_("vec"))], inc="vec")
            E_("scalar", lambda e: e.activation(nrm[0:1, 96:112], nrm[0:1, 96:112], AF.Sqrt), waits=[("vec", n_("vec"))], inc="sact")
            E_("scalar", lambda e: e.activation(nrm[0:1, 91:92], nrm[0:1, 91:92], AF.Sqrt), inc="sact")
            E_("scalar", lambda e: e.activation(nrm[0:1, 16:80], nrm[0:1, 16:80], AF.Sqrt), waits=[("vec", P2_V)], inc="sact")
            nsq = n_("sact")
            E_("vector", lambda e: e.tensor_scalar(nrm[0:1, 96:112], nrm[0:1, 96:112], nrm[0:1, 91:92], None, op0=ALU.add),
               waits=[("sact", nsq)], inc="vec")
            for h in range(NHEAD):
                E_("vector", lambda e, h=h: e.tensor_scalar(nrm[0:1, 16 + 4 * h:16 + 4 * h + 4], nrm[0:1, 16 + 4 * h:16 + 4 * h + 4],
                                                            nrm[0:1, 96 + h:97 + h], None, op0=ALU.mult), inc="vec")
            E_("vector", lambda e: e.tensor_scalar(nrm[0:1, 16:80], nrm[0:1, 16:80], -0.125, None, op0=ALU.mult), inc="vec")
            ncv = n_("vec")
            nb5 = bank(5, 0, 64)
            E_("tensor", lambda e: e.matmul(nb5, ones1[0:1, :], nrm[0:1, 16:80], start=True, stop=True),
               waits=[("vec", ncv), ("tmm", P2_T)], inc="tmm")
            E_("vector", lambda e: e.tensor_copy(negc[:, :], nb5), waits=[("tmm", n_("tmm"))], inc="vec")
            NEGC_V = n_("vec")

        for h in range(NHEADS_RUN):
            par = h % 2
            KREL, QQ = KRELs[par], QQs[par]
            prev_sc = M.get(("sc_last", h - 2), 0)
            ksem = f"kv{par}"
            E_("sync", lambda e, h=h, KREL=KREL: e.dma_start(out=KREL[0:64, 0:2048], in_=agk_dst[h * 64:(h + 1) * 64, :]),
               waits=[("tmm", prev_sc), ("cck", 1), ("rel", REL_KVQ)], inc=ksem, incn=16)
            E_("sync", lambda e, h=h, KREL=KREL: e.dma_start(out=KREL[0:64, 2048:4096],
                                                             in_=agk_dst[1024 + h * 64: 1024 + (h + 1) * 64, :]),
               waits=[("cck", 1)], inc=ksem, incn=16)
            E_("sync", lambda e, h=h, QQ=QQ: e.dma_start(out=QQ[0:64, :], in_=qsc[h * 64:(h + 1) * 64, :]),
               waits=[("tmm", M.get(("sc_last", h - 1), 0))] + Q_ST_W, inc=ksem, incn=16)
            E_("sync", lambda e, h=h, QQ=QQ: e.dma_start(out=QQ[64:128, :], in_=qsc[h * 64:(h + 1) * 64, :]),
               inc=ksem, incn=16)
            prev_av = M.get(("av_last", h - 1), 0)
            for hh in range(2):
                vv_ = Vhalf[hh].rearrange("p (c s) -> p c s", s=65)
                vsrc = agv_dst[hh * 2048:(hh + 1) * 2048, h * 64:(h + 1) * 64].rearrange("(c p) s -> p c s", p=128)
                E_("sync", lambda e, vv_=vv_, vsrc=vsrc: e.dma_start(out=vv_[:, :, 0:64], in_=vsrc),
                   waits=[("tmm", prev_av), ("ccv", 1), ("vec", ONES2_V)], inc=ksem, incn=16)
            kvq_h = (ksem, n_(ksem))
            for tc in range(4):
                avb = psum[:, 4 * 512:5 * 512]
                prev_ev = M.get(("avev", h, tc - 1)) or M.get(("avev", h - 1, 3), 0)
                prev_rc = M.get(("avrc", h, tc - 1)) or M.get(("avrc", h - 1, 3), 0)
                for kc in range(32):
                    sb = bank(kc % 4)
                    et = expT[kc % 4]
                    wv = [kvq_h, ("vec", NEGC_V)]
                    pe = M.get(("exp", h, tc, kc - 4)) or M.get(("exp", h, tc - 1, 28 + kc)) or M.get(("exp", h - 1, 3, 28 + kc), 0)
                    if pe:
                        wv.append(("sact", pe))
                    E_("tensor", lambda e, sb=sb, kc=kc, tc=tc, KREL=KREL, QQ=QQ:
                       e.matmul(sb, KREL[:, kc * 128:(kc + 1) * 128], QQ[:, tc * 512:(tc + 1) * 512], start=True, stop=True),
                       waits=wv, inc="tmm")
                    M[("sc_last", h)] = n_("tmm")
                    sct = n_("tmm")
                    wv2 = [("tmm", sct)]
                    pavm = M.get(("avmm", h, tc, kc - 4))
                    if pavm:
                        wv2.append(("tmm", pavm))
                    ncol = negc[:, h * 4 + tc: h * 4 + tc + 1]
                    E_("scalar", lambda e, et=et, sb=sb, ncol=ncol: e.activation(et, sb, AF.Exp, bias=ncol, scale=0.125),
                       waits=wv2, inc="sact")
                    M[("exp", h, tc, kc)] = n_("sact")
                    expc = n_("sact")
                    if P3_DO_AV:
                        wv3 = [("sact", expc)]
                        if kc == 0:
                            wv3 += [("vec", prev_ev), ("vec", prev_rc)]
                        vsl = Vhalf[kc // 16][:, (kc % 16) * 65:(kc % 16) * 65 + 65]
                        E_("tensor", lambda e, vsl=vsl, et=et, kc=kc:
                           e.matmul(avb[0:65, :], vsl, et, start=(kc == 0), stop=(kc == 31)),
                           waits=wv3, inc="tmm")
                        M[("avmm", h, tc, kc)] = n_("tmm")
                M[("av_last", h)] = n_("tmm")
                avd = n_("tmm")
                if not (P3_DO_AV and P3_DO_EVAC):
                    M[("avev", h, tc)] = n_("vec")
                    M[("avrc", h, tc)] = n_("vec")
                    M[("avbc", h, tc)] = n_("vec")
                    continue
                E_("vector", lambda e: e.reciprocal(stagsq[64:65, :], avb[64:65, :]),
                   waits=[("tmm", avd), ("tmm", M.get(("avbcT", h, tc - 1)) or M.get(("avbcT", h - 1, 3), 0))], inc="vec")
                M[("avrc", h, tc)] = n_("vec")
                rc = n_("vec")
                bc = bank(5, 0, 512)
                E_("tensor", lambda e, bc=bc: e.matmul(bc[0:64, :], onesr[64:65, 0:64], stagsq[64:65, :], start=True, stop=True),
                   waits=[("vec", rc), ("vec", M.get(("avbc", h, tc - 1)) or M.get(("avbc", h - 1, 3), NEGC_V))],
                   inc="tmm")
                M[("avbcT", h, tc)] = n_("tmm")
                bct = n_("tmm")
                hp = h // 2
                dstcols = slice(hp * 2048 + tc * 512, hp * 2048 + (tc + 1) * 512)
                bvcol = bvT[0:64, h:h + 1]
                if par == 0:
                    E_("vector", lambda e: e.tensor_copy(stgo[0:64, :], avb[0:64, :]),
                       waits=[("tmm", bct), ("sta", M.get(("avst", h - 1, 3), 0))], inc="vec")
                    E_("vector", lambda e, bc=bc: e.tensor_mul(stgo[0:64, :], stgo[0:64, :], bc[0:64, :]), inc="vec")
                    E_("vector", lambda e, dstcols=dstcols, bvcol=bvcol:
                       e.tensor_scalar(attnT[0:64, dstcols], stgo[0:64, :], bvcol, None, op0=ALU.add), inc="vec")
                else:
                    pst = M.get(("avst", h, tc - 1)) or M.get(("avst", h - 2, 3), 0)
                    E_("vector", lambda e: e.tensor_copy(stgo[0:64, :], avb[0:64, :]),
                       waits=[("tmm", bct), ("sta", pst)], inc="vec")
                    E_("vector", lambda e, bc=bc: e.tensor_mul(stgo[0:64, :], stgo[0:64, :], bc[0:64, :]), inc="vec")
                    E_("vector", lambda e, bvcol=bvcol: e.tensor_scalar(stgo[0:64, :], stgo[0:64, :], bvcol, None, op0=ALU.add), inc="vec")
                M[("avbc", h, tc)] = n_("vec")
                M[("avev", h, tc)] = n_("vec")
                if par == 1:
                    ve = n_("vec")
                    E_("gpsimd", lambda e, dstcols=dstcols: e.dma_start(out=attnT[64:128, dstcols], in_=stgo[0:64, :]),
                       waits=[("vec", ve)], inc="sta", incn=16)
                    M[("avst", h, tc)] = n_("sta")
        P3_T, P3_V, P3_ST, P3_SA = n_("tmm"), n_("vec"), n_("sta"), n_("sact")
        if DEBUG:
            E_("gpsimd", lambda e: e.dma_start(out=dbg["dbg_attnT"].rearrange("(k p) t -> p k t", p=128),
                                               in_=attnT.ap().rearrange("p (k t) -> p k t", t=2048)),
               waits=[("vec", P3_V), ("sta", P3_ST)], inc="dbg2", incn=16)

    P4_V = 0
    if PHASE_LIMIT >= 4:
        woutsb = ybuf.ap()[:, 0:8192].rearrange("p (k o) -> p k o", o=1024)
        E_("sync", lambda e: e.dma_start(out=woutsb, in_=wout_d.rearrange("(k p) o -> p k o", p=128)),
           waits=[("tmm", P3_T)], inc="wol", incn=16)
        E_("sync", lambda e: e.dma_start(out=boutrow, in_=bout),
           waits=[("tmm", P3_T)] + Q_ST_W, inc="wol", incn=16)
        WOL = n_("wol")
        for i in range(NT):
            bp = 2 * (i % 2)
            acc = psum[:, bp * 512:(bp + 2) * 512]
            wv = [("wol", WOL), ("vec", P3_V), ("sta", P3_ST), ("sact", P3_SA), ("dbg1", M.get("dbg1", 0))]
            if i >= 2:
                wv.append(("vec", M[("opv", i - 2)]))
            first = True
            for hp in range(8):
                for nn2 in range(2):
                    E_("tensor", lambda e, hp=hp, nn2=nn2, i=i, acc=acc:
                       e.matmul(acc[:, nn2 * 512:(nn2 + 1) * 512],
                                attnT[:, hp * 2048 + i * 128: hp * 2048 + (i + 1) * 128],
                                woutsb[:, hp, nn2 * 512:(nn2 + 1) * 512], start=(hp == 0), stop=False),
                       waits=wv if first else (), inc=None)
                    first = False
            for nn2 in range(2):
                E_("tensor", lambda e, nn2=nn2, acc=acc: e.matmul(acc[:, nn2 * 512:(nn2 + 1) * 512], ones1[0:1, :],
                                                                  boutrow[0:1, nn2 * 512:(nn2 + 1) * 512], start=False, stop=True),
                   inc="tmm" if nn2 == 1 else None)
            opt = n_("tmm")
            xi = wbuf[:, i * 1024:(i + 1) * 1024]
            E_("vector", lambda e, xi=xi, acc=acc: e.tensor_add(xi, xi, acc), waits=[("tmm", opt)], inc="vec")
            M[("opv", i)] = n_("vec")
        P4_V = n_("vec")
        if DEBUG:
            E_("gpsimd", lambda e: e.dma_start(out=dbg["dbg_x2"].rearrange("(i p) c -> p i c", p=128),
                                               in_=wbuf.ap().rearrange("p (i c) -> p i c", c=1024)),
               waits=[("vec", P4_V)], inc="dbg3", incn=16)

    if PHASE_LIMIT >= 5:
        ffn_phase(wbuf, xbuf, ybuf,
                  [("vec", P4_V), ("tmm", n_("tmm")), ("sact", n_("sact")),
                   ("st0", n_("st0")), ("st1", n_("st1")), ("stv", n_("stv")), ("sta", n_("sta")),
                   ("dbg2", n_("dbg2")), ("dbg3", n_("dbg3"))])
    P_LAST_V = n_("vec")

    src = wbuf if PHASE_LIMIT >= 4 else xbuf
    E_("gpsimd", lambda e: e.dma_start(out=y_out.rearrange("(i p) c -> p i c", p=128),
                                       in_=src.ap().rearrange("p (i c) -> p i c", c=1024)),
       waits=[("vec", P_LAST_V), ("tmm", n_("tmm")), ("sact", n_("sact"))], inc="sty", incn=16)
    FIN = n_("sty")
    E_("gpsimd", lambda e: e.nop(), waits=[("sty", FIN)], inc=None)

    sch.check_runnable()
    semctx = []
    sems = {}
    for s in sorted(sch.semnames):
        hd = nc.semaphore(s)
        sems[s] = hd.__enter__()
        semctx.append(hd)
    sch.realize(nc, sems)
    for hd in reversed(semctx):
        hd.__exit__(None, None, None)
    for hd in reversed(ctxs):
        hd.__exit__(None, None, None)
    return nc


def prepare_in_maps(L0_attn_x, L0_attn_Wqkv, L0_attn_bqkv, L0_attn_Wout, L0_attn_bout,
                    L0_attn_rel_emb, L0_ffn_W, L0_ffn_b, L0_ffn_Wg, L0_ffn_bg):
    f32 = np.float32
    shared = {
        "wqkv": np.ascontiguousarray(L0_attn_Wqkv, dtype=f32),
        "bqkv": np.ascontiguousarray(L0_attn_bqkv, dtype=f32).reshape(1, 3 * D),
        "wout": np.ascontiguousarray(L0_attn_Wout, dtype=f32),
        "bout": np.ascontiguousarray(L0_attn_bout, dtype=f32).reshape(1, D),
        "relpt": np.ascontiguousarray(
            np.concatenate([np.zeros((1, DH), f32), np.asarray(L0_attn_rel_emb, f32)], axis=0).T),
        "wt": np.ascontiguousarray(np.asarray(L0_ffn_W, f32).transpose(0, 2, 1)),
        "bffn": np.ascontiguousarray(L0_ffn_b, dtype=f32),
        "wgt": np.ascontiguousarray(np.asarray(L0_ffn_Wg, f32).T),
        "bg": np.ascontiguousarray(L0_ffn_bg, dtype=f32).reshape(1, E),
        "ident": np.eye(128, dtype=f32),
    }
    x = np.asarray(L0_attn_x, f32)
    in_maps = []
    for c in range(8):
        b, hf = c // 2, c % 2
        m = dict(shared)
        m["x"] = np.ascontiguousarray(x[b, hf * TPC:(hf + 1) * TPC, :])
        in_maps.append(m)
    return in_maps


def kernel(L0_attn_x, L0_attn_Wqkv, L0_attn_bqkv, L0_attn_Wout, L0_attn_bout,
           L0_attn_rel_emb, L0_ffn_W, L0_ffn_b, L0_ffn_Wg, L0_ffn_bg):
    in_maps = prepare_in_maps(L0_attn_x, L0_attn_Wqkv, L0_attn_bqkv, L0_attn_Wout, L0_attn_bout,
                              L0_attn_rel_emb, L0_ffn_W, L0_ffn_b, L0_ffn_Wg, L0_ffn_bg)
    nc = build_nc()
    res = run_bass_kernel_spmd(nc, in_maps, core_ids=list(range(8)))
    out = np.empty((B, S, D), np.float32)
    for c in range(8):
        b, hf = c // 2, c % 2
        out[b, hf * TPC:(hf + 1) * TPC, :] = res.results[c]["y"]
    return out
